# version 04
# baseline (speedup 1.0000x reference)
import numpy as np
import ml_dtypes
from contextlib import ExitStack
import concourse.bass as bass
import concourse.mybir as mybir
from concourse.bass_utils import run_bass_kernel_spmd

F32 = mybir.dt.float32
BF16 = mybir.dt.bfloat16
AF = mybir.ActivationFunctionType
ALU = mybir.AluOpType
AX = mybir.AxisListType.X
BF = ml_dtypes.bfloat16

B, N, D, DFF = 4, 2048, 2048, 5504
NH, HD = 16, 128
NC = 8
RPC = 1024
NIT = D // 128
NDT = DFF // 128
EPS = 1e-5
BMAG = float(1.5 * 2 ** 23)
SCL = float(1.0 / np.float32(np.sqrt(np.float32(128.0))))
CW = 344
NCH = DFF // CW
POS_EVEN = [0, 3, 4, 7, 8, 11, 12, 15]
POS_ODD = [1, 2, 5, 6, 9, 10, 13, 14]
MAXCNT = [8, 8, 7, 7, 6, 6, 5, 5, 4, 4, 3, 3, 2, 2, 1, 1]
KT_OWNER = [(0 if (k % 4) in (0, 3) else 1) for k in range(16)]
KT_J = [(POS_EVEN if KT_OWNER[k] == 0 else POS_ODD).index(k) for k in range(16)]
KV_HALF = 4 * 128 * 1024


class Sem:
    def __init__(self, es, nc, name):
        self.h = es.enter_context(nc.semaphore(name))
        self.v = 0

    def inc(self, n=1):
        self.v += n
        return self.v


def build_program():
    nc = bass.Bass("TRN2", target_bir_lowering=False, debug=False,
                   detect_race_conditions=False)
    d_x = nc.dram_tensor("x_rows", [RPC, D], F32, kind="ExternalInput").ap()
    d_wg = nc.dram_tensor("wg_s", [256, DFF], F32, kind="ExternalInput").ap()
    d_wu = nc.dram_tensor("wu_s", [256, DFF], F32, kind="ExternalInput").ap()
    d_wd = nc.dram_tensor("wd_s", [DFF, 256], F32, kind="ExternalInput").ap()
    d_Wq = nc.dram_tensor("Wq", [D, D], F32, kind="ExternalInput").ap()
    d_Wk = nc.dram_tensor("Wk", [D, 512], F32, kind="ExternalInput").ap()
    d_Wv = nc.dram_tensor("Wv", [D, 512], F32, kind="ExternalInput").ap()
    d_Wo = nc.dram_tensor("Wo", [D, D], F32, kind="ExternalInput").ap()
    d_bo = nc.dram_tensor("bo", [1, D], F32, kind="ExternalInput").ap()
    d_msk = nc.dram_tensor("masks", [16, 2, 128, 128], BF16, kind="ExternalInput").ap()
    d_eye = nc.dram_tensor("eye", [128, 128], BF16, kind="ExternalInput").ap()
    d_on128 = nc.dram_tensor("ones128", [128, 128], BF16, kind="ExternalInput").ap()
    d_onKK = nc.dram_tensor("onesKK", [128, 128], F32, kind="ExternalInput").ap()
    d_on1 = nc.dram_tensor("ones1", [1, 128], F32, kind="ExternalInput").ap()
    d_sel = nc.dram_tensor("sel16", [16, 16 * 128], F32, kind="ExternalInput").ap()
    d_y = nc.dram_tensor("y_rows", [RPC, D], F32, kind="ExternalOutput").ap()
    d_agg_i = nc.dram_tensor("agg_i", [256, DFF], BF16)
    d_agg_o = nc.dram_tensor("agg_o", [D, DFF], BF16)
    d_agu_i = nc.dram_tensor("agu_i", [256, DFF], BF16)
    d_agu_o = nc.dram_tensor("agu_o", [D, DFF], BF16)
    d_agd_i = nc.dram_tensor("agd_i", [DFF, 256], BF16)
    d_agd_o = nc.dram_tensor("agd_o", [NC * DFF, 256], BF16)
    d_kv_i = nc.dram_tensor("kv_i", [1, 2 * KV_HALF], BF16)
    d_kv_o = nc.dram_tensor("kv_o", [2, 2 * KV_HALF], BF16)
    d_dum_i = nc.dram_tensor("dum_i", [1, 128], BF16)
    d_dum_o = nc.dram_tensor("dum_o", [NC, 128], BF16)
    d_dmk_i = nc.dram_tensor("dmk_i", [1, 128], BF16)
    d_dmk_o = nc.dram_tensor("dmk_o", [2, 128], BF16)
    d_x1 = nc.dram_tensor("x1s", [RPC, D], F32)
    d_x2 = nc.dram_tensor("x2s", [RPC, D], F32)
    import os as _os2
    _dbgenv = _os2.environ.get("KDBG", "")
    _dbg = bool(_dbgenv)
    _dbgfull = _dbgenv == "1"
    if _dbg:
        d_dbg_x1 = nc.dram_tensor("dbg_x1", [RPC, D], F32, kind="ExternalOutput").ap()
        d_dbg_x2 = nc.dram_tensor("dbg_x2", [RPC, D], F32, kind="ExternalOutput").ap()
        d_dbg_wg = nc.dram_tensor("dbg_wg", [D, DFF], BF16, kind="ExternalOutput").ap()
        d_dbg_wd = nc.dram_tensor("dbg_wd", [NC * DFF, 256], BF16, kind="ExternalOutput").ap()
        d_dbg_xi = nc.dram_tensor("dbg_xi", [128, NIT * RPC], BF16, kind="ExternalOutput").ap()
        d_dbg_st = nc.dram_tensor("dbg_st", [128, 32], F32, kind="ExternalOutput").ap()
        d_dbg_xt = nc.dram_tensor("dbg_xt", [128, D], BF16, kind="ExternalOutput").ap()
        d_dbg_ub = nc.dram_tensor("dbg_ub", [128, D], F32, kind="ExternalOutput").ap()
        d_dbg_xs = nc.dram_tensor("dbg_xs", [128, 4 * D], F32, kind="ExternalOutput").ap()

    es = ExitStack()
    uid = [0]

    def mk(st, shape, dt, psum=False):
        uid[0] += 1
        name = f"t{uid[0]}"
        if psum:
            return st.enter_context(nc.psum_tensor(name, shape, dt))
        return st.enter_context(nc.sbuf_tensor(name, shape, dt))

    c_eye = mk(es, [128, 128], BF16)
    c_on128 = mk(es, [128, 128], BF16)
    c_onKK = mk(es, [128, 128], F32)
    c_on1 = mk(es, [1, 128], F32)
    c_bo = mk(es, [128, D], F32)
    st_sum = mk(es, [128, 8], F32)
    st_sq = mk(es, [128, 8], F32)
    st_max = mk(es, [128, 8], F32)
    st_min = mk(es, [128, 8], F32)
    st_a = mk(es, [128, 8], F32)
    st_b = mk(es, [128, 8], F32)
    st_mean = mk(es, [128, 8], F32)
    st_istd = mk(es, [128, 8], F32)
    st_c = mk(es, [128, 8], F32)
    st_d = mk(es, [128, 8], F32)
    st_scl = mk(es, [128, 8], F32)
    st_bia = mk(es, [128, 8], F32)
    st_inv = mk(es, [128, 8], F32)
    h_s2 = mk(es, [128, 8], F32)
    h_hi = mk(es, [128, 8], F32)

    sLd = Sem(es, nc, "sLd")
    sLg = Sem(es, nc, "sLg")
    sT = Sem(es, nc, "sT")
    sV = Sem(es, nc, "sV")
    sS = Sem(es, nc, "sS")
    sCg = Sem(es, nc, "sCg")
    sCu = Sem(es, nc, "sCu")
    sCd = Sem(es, nc, "sCd")
    sCx = Sem(es, nc, "sCx")
    sCk = Sem(es, nc, "sCk")
    sCy = Sem(es, nc, "sCy")

    ops = {k: [] for k in ("sync", "gpsimd", "tensor", "vector", "scalar")}

    def em(eng, fn, _fence=True):
        if _fence and eng == "vector" and sV.v > 0:
            ops[eng].append(lambda e, v=sV.v: e.wait_ge(sV.h, v))
        if _fence and eng == "scalar" and sS.v > 0:
            ops[eng].append(lambda e, v=sS.v: e.wait_ge(sS.h, v))
        ops[eng].append(fn)

    def wait(eng, sem, val):
        if val > 0:
            em(eng, lambda e, s=sem, v=val: e.wait_ge(s.h, v), _fence=False)

    def barrier():
        vals = [(sT, sT.v), (sV, sV.v), (sS, sS.v), (sLd, sLd.v), (sLg, sLg.v)]
        for eng in ("tensor", "vector", "scalar", "sync", "gpsimd"):
            for sem, v in vals:
                wait(eng, sem, v)

    def p_consts(st):
        def f(e):
            e.dma_start(out=c_eye[:], in_=d_eye).then_inc(sLd.h, 16)
            e.dma_start(out=c_on128[:], in_=d_on128).then_inc(sLd.h, 16)
            e.dma_start(out=c_onKK[:], in_=d_onKK).then_inc(sLd.h, 16)
            e.dma_start(out=c_on1[:], in_=d_on1).then_inc(sLd.h, 16)
        em("sync", f)
        sLd.inc(64)
        bo_row = mk(st, [1, D], F32)
        em("sync", lambda e: e.dma_start(out=bo_row[:], in_=d_bo).then_inc(sLd.h, 16))
        sLd.inc(16)
        pbo = mk(st, [128, 2 * 512], F32, psum=True)
        wait("tensor", sLd, sLd.v)
        ev = {}
        for ch in range(4):
            par = ch % 2
            if ch >= 2:
                wait("tensor", sV, ev[ch - 2])
            em("tensor", lambda e, ch=ch, par=par: e.matmul(
                pbo[:, par * 512:(par + 1) * 512], c_on1[:],
                bo_row[:, 512 * ch:512 * (ch + 1)], start=True, stop=True).then_inc(sT.h, 1))
            sT.inc()
            wait("vector", sT, sT.v)
            em("vector", lambda e, ch=ch, par=par: e.tensor_copy(
                out=c_bo[:, 512 * ch:512 * (ch + 1)],
                in_=pbo[:, par * 512:(par + 1) * 512]).then_inc(sV.h, 1))
            sV.inc()
            ev[ch] = sV.v

    def p_quant(st):
        q_ws = mk(st, [128, 2 * DFF], F32)
        q_abs = mk(st, [128, 2 * DFF], F32)
        q_half = mk(st, [128, 2 * DFF], F32)
        q_wq = mk(st, [128, 2 * DFF], BF16)
        pq = mk(st, [128, 2 * 512], F32, psum=True)

        def one_weight(d_in, d_ag_in, d_ag_out, groups, nbig, ccsem):
            ntot = groups * nbig
            nds = max(1, (groups * 128) // 1024)
            wait("sync", sV, sV.v)
            for s in range(nds):
                g0, g1 = s * groups // nds, (s + 1) * groups // nds
                em("sync", lambda e, g0=g0, g1=g1, nbig=nbig, d_in=d_in: e.dma_start(
                    out=q_ws[:].rearrange("p (t c) -> p t c", c=nbig)[:, g0:g1],
                    in_=d_in.rearrange("(t p) c -> p t c", p=128)[:, g0:g1]
                ).then_inc(sLd.h, 16))
                sLd.inc(16)
            wait("scalar", sLd, sLd.v)
            wait("scalar", sV, sV.v)
            wait("scalar", sLg, sLg.v)
            em("scalar", lambda e, ntot=ntot: e.activation(
                out=q_abs[:, 0:ntot], in_=q_ws[:, 0:ntot], func=AF.Abs).then_inc(sS.h, 1))
            sS.inc()
            abs_done = sS.v
            em("scalar", lambda e, ntot=ntot: e.activation(
                out=q_ws[:, 0:ntot], in_=q_ws[:, 0:ntot], func=AF.Sign).then_inc(sS.h, 1))
            sS.inc()
            sign_done = sS.v
            wait("tensor", sS, abs_done)
            wait("tensor", sV, sV.v)
            wait("vector", sLg, sLg.v)
            vd = {}
            k = 0
            ncpg = (nbig + 511) // 512
            for g in range(groups):
                for ci in range(ncpg):
                    c0 = ci * 512
                    cw_ = min(512, nbig - c0)
                    par = k % 2
                    if k >= 2:
                        wait("tensor", sV, vd[k - 2])
                    em("tensor", lambda e, g=g, c0=c0, cw_=cw_, par=par, nbig=nbig: e.matmul(
                        pq[:, par * 512: par * 512 + cw_], c_onKK[:],
                        q_abs[:, g * nbig + c0: g * nbig + c0 + cw_],
                        start=True, stop=True).then_inc(sT.h, 1))
                    sT.inc()
                    wait("vector", sT, sT.v)
                    em("vector", lambda e, g=g, c0=c0, cw_=cw_, par=par, nbig=nbig: e.tensor_scalar(
                        out=q_half[:, g * nbig + c0: g * nbig + c0 + cw_],
                        in0=pq[:, par * 512: par * 512 + cw_], scalar1=1.0 / 256,
                        scalar2=5e-6, op0=ALU.mult, op1=ALU.add).then_inc(sV.h, 1))
                    sV.inc()
                    vd[k] = sV.v
                    k += 1
            em("vector", lambda e, ntot=ntot: e.tensor_tensor(
                out=q_abs[:, 0:ntot], in0=q_abs[:, 0:ntot], in1=q_half[:, 0:ntot],
                op=ALU.is_gt).then_inc(sV.h, 1))
            sV.inc()
            wait("vector", sS, sign_done)
            em("vector", lambda e, ntot=ntot: e.tensor_tensor(
                out=q_abs[:, 0:ntot], in0=q_abs[:, 0:ntot], in1=q_ws[:, 0:ntot],
                op=ALU.mult).then_inc(sV.h, 1))
            sV.inc()
            em("vector", lambda e, ntot=ntot: e.scalar_tensor_tensor(
                out=q_wq[:, 0:ntot], in0=q_abs[:, 0:ntot], scalar=2.0,
                in1=q_half[:, 0:ntot], op0=ALU.mult, op1=ALU.mult).then_inc(sV.h, 1))
            sV.inc()
            wait("gpsimd", sV, sV.v)
            for s in range(nds):
                g0, g1 = s * groups // nds, (s + 1) * groups // nds
                em("gpsimd", lambda e, g0=g0, g1=g1, nbig=nbig, d_ag_in=d_ag_in: e.dma_start(
                    out=d_ag_in.rearrange("(t p) c -> p t c", p=128)[:, g0:g1],
                    in_=q_wq[:].rearrange("p (t c) -> p t c", c=nbig)[:, g0:g1]
                ).then_inc(sLg.h, 16))
                sLg.inc(16)
            wait("gpsimd", sLg, sLg.v)
            em("gpsimd", lambda e, ccsem=ccsem, d_ag_in=d_ag_in, d_ag_out=d_ag_out:
               e.collective_compute(
                   "AllGather", ALU.bypass, replica_groups=[list(range(NC))],
                   ins=[d_ag_in.ap().opt()], outs=[d_ag_out.ap().opt()]).then_inc(ccsem.h, 1))
            ccsem.inc()

        one_weight(d_wg, d_agg_i, d_agg_o, 2, DFF, sCg)
        one_weight(d_wu, d_agu_i, d_agu_o, 2, DFF, sCu)
        one_weight(d_wd, d_agd_i, d_agd_o, NDT, 256, sCd)
        em("gpsimd", lambda e: e.collective_compute(
            "AllGather", ALU.bypass, replica_groups=[list(range(NC))],
            ins=[d_dum_i.ap().opt()], outs=[d_dum_o.ap().opt()]).then_inc(sCx.h, 1))
        sCx.inc()

    def p_ln(d_src, quant, dstT, tap=False):
        st = ExitStack()
        x_sb = mk(st, [128, 8 * D], F32)
        x2t = mk(st, [128, D], F32)
        ub = mk(st, [128, 2 * D], F32)
        xt = mk(st, [128, 2 * D], BF16)
        ptr = mk(st, [128, 2 * 1024], BF16, psum=True)
        for rt in range(8):
            em("gpsimd", lambda e, rt=rt: e.dma_start(
                out=x_sb[:, rt * D:(rt + 1) * D],
                in_=d_src[rt * 128:(rt + 1) * 128, :]).then_inc(sLg.h, 16))
            sLg.inc(16)
        ld_done = sLg.v
        sq_done = {}
        wait("vector", sLg, ld_done)
        wait("scalar", sLg, ld_done)
        for rt in range(8):
            xr = x_sb[:, rt * D:(rt + 1) * D]
            em("vector", lambda e, xr=xr, rt=rt: e.tensor_reduce(
                out=st_sum[:, rt:rt + 1], in_=xr, axis=AX, op=ALU.add).then_inc(sV.h, 1))
            sV.inc()
            wait("vector", sV, sV.v)
            if quant:
                em("vector", lambda e, xr=xr, rt=rt: e.tensor_reduce(
                    out=st_max[:, rt:rt + 1], in_=xr, axis=AX, op=ALU.max).then_inc(sV.h, 1))
                sV.inc()
                wait("vector", sV, sV.v)
                em("vector", lambda e, xr=xr, rt=rt: e.tensor_reduce(
                    out=st_min[:, rt:rt + 1], in_=xr, axis=AX, op=ALU.min).then_inc(sV.h, 1))
                sV.inc()
                wait("vector", sV, sV.v)
            if rt >= 1:
                wait("scalar", sV, sq_done[rt - 1])
            em("scalar", lambda e, xr=xr, rt=rt: e.activation(
                out=x2t[:], in_=xr,
                func=AF.Square).then_inc(sS.h, 1))
            sS.inc()
            wait("vector", sS, sS.v)
            em("vector", lambda e, rt=rt: e.tensor_reduce(
                out=st_sq[:, rt:rt + 1], in_=x2t[:],
                axis=AX, op=ALU.add).then_inc(sV.h, 1))
            sV.inc()
            wait("vector", sV, sV.v)
            sq_done[rt] = sV.v

        def V(f):
            em("vector", f)
            sV.inc()
            wait("vector", sV, sV.v)

        V(lambda e: e.tensor_scalar(out=st_mean[:], in0=st_sum[:], scalar1=1.0 / D,
                                    scalar2=None, op0=ALU.mult).then_inc(sV.h, 1))
        V(lambda e: e.tensor_scalar(out=st_a[:], in0=st_sq[:], scalar1=1.0 / D,
                                    scalar2=None, op0=ALU.mult).then_inc(sV.h, 1))
        V(lambda e: e.tensor_tensor(out=st_b[:], in0=st_mean[:], in1=st_mean[:],
                                    op=ALU.mult).then_inc(sV.h, 1))
        V(lambda e: e.tensor_tensor(out=st_a[:], in0=st_a[:], in1=st_b[:],
                                    op=ALU.subtract).then_inc(sV.h, 1))
        V(lambda e: e.tensor_scalar(out=st_a[:], in0=st_a[:], scalar1=EPS,
                                    scalar2=None, op0=ALU.add).then_inc(sV.h, 1))
        wait("scalar", sV, sV.v)
        em("scalar", lambda e: e.activation(out=st_b[:], in_=st_a[:],
                                            func=AF.Ln).then_inc(sS.h, 1))
        sS.inc()
        em("scalar", lambda e: e.activation(out=st_c[:], in_=st_b[:],
                                            func=AF.Exp, scale=-0.5).then_inc(sS.h, 1))
        sS.inc()
        wait("vector", sS, sS.v)
        V(lambda e: e.tensor_tensor(out=st_d[:], in0=st_c[:], in1=st_c[:],
                                    op=ALU.mult).then_inc(sV.h, 1))
        V(lambda e: e.tensor_tensor(out=st_d[:], in0=st_d[:], in1=st_a[:],
                                    op=ALU.mult).then_inc(sV.h, 1))
        V(lambda e: e.tensor_scalar(out=st_d[:], in0=st_d[:], scalar1=-0.5,
                                    scalar2=1.5, op0=ALU.mult, op1=ALU.add).then_inc(sV.h, 1))
        V(lambda e: e.tensor_tensor(out=st_istd[:], in0=st_c[:], in1=st_d[:],
                                    op=ALU.mult).then_inc(sV.h, 1))
        if quant:
            V(lambda e: e.tensor_tensor(out=st_a[:], in0=st_max[:], in1=st_mean[:],
                                        op=ALU.subtract).then_inc(sV.h, 1))
            V(lambda e: e.tensor_tensor(out=st_b[:], in0=st_mean[:], in1=st_min[:],
                                        op=ALU.subtract).then_inc(sV.h, 1))
            V(lambda e: e.tensor_tensor(out=st_a[:], in0=st_a[:], in1=st_b[:],
                                        op=ALU.max).then_inc(sV.h, 1))
            V(lambda e: e.tensor_tensor(out=st_a[:], in0=st_a[:], in1=st_istd[:],
                                        op=ALU.mult).then_inc(sV.h, 1))
            V(lambda e: e.tensor_scalar(out=st_a[:], in0=st_a[:], scalar1=EPS,
                                        scalar2=None, op0=ALU.max).then_inc(sV.h, 1))
            V(lambda e: e.tensor_scalar(out=st_inv[:], in0=st_a[:], scalar1=1.0 / 127,
                                        scalar2=None, op0=ALU.mult).then_inc(sV.h, 1))
            wait("scalar", sV, sV.v)
            em("scalar", lambda e: e.activation(out=st_b[:], in_=st_a[:],
                                                func=AF.Ln).then_inc(sS.h, 1))
            sS.inc()
            em("scalar", lambda e: e.activation(out=st_c[:], in_=st_b[:],
                                                func=AF.Exp, scale=-1.0).then_inc(sS.h, 1))
            sS.inc()
            wait("vector", sS, sS.v)
            V(lambda e: e.tensor_tensor(out=st_d[:], in0=st_c[:], in1=st_a[:],
                                        op=ALU.mult).then_inc(sV.h, 1))
            V(lambda e: e.tensor_scalar(out=st_d[:], in0=st_d[:], scalar1=-1.0,
                                        scalar2=2.0, op0=ALU.mult, op1=ALU.add).then_inc(sV.h, 1))
            V(lambda e: e.tensor_tensor(out=st_b[:], in0=st_c[:], in1=st_d[:],
                                        op=ALU.mult).then_inc(sV.h, 1))
            V(lambda e: e.tensor_scalar(out=st_b[:], in0=st_b[:], scalar1=127.0,
                                        scalar2=None, op0=ALU.mult).then_inc(sV.h, 1))
            V(lambda e: e.tensor_tensor(out=st_scl[:], in0=st_b[:], in1=st_istd[:],
                                        op=ALU.mult).then_inc(sV.h, 1))
            V(lambda e: e.tensor_tensor(out=st_b[:], in0=st_scl[:], in1=st_mean[:],
                                        op=ALU.mult).then_inc(sV.h, 1))
            V(lambda e: e.tensor_scalar(out=st_bia[:], in0=st_b[:], scalar1=-1.0,
                                        scalar2=BMAG, op0=ALU.mult, op1=ALU.add).then_inc(sV.h, 1))
        else:
            V(lambda e: e.tensor_tensor(out=st_b[:], in0=st_istd[:], in1=st_mean[:],
                                        op=ALU.mult).then_inc(sV.h, 1))
            V(lambda e: e.tensor_scalar(out=st_bia[:], in0=st_b[:], scalar1=-1.0,
                                        scalar2=None, op0=ALU.mult).then_inc(sV.h, 1))
        stats_done = sV.v
        wait("scalar", sV, stats_done)
        tdone = {}
        vexp = {}
        evd = {}
        for rt in range(8):
            xr = x_sb[:, rt * D:(rt + 1) * D]
            par = rt % 2
            if quant:
                if rt >= 2:
                    wait("vector", sT, tdone[rt - 2])
                    wait("vector", sV, vexp[rt - 2])
                em("vector", lambda e, xr=xr, rt=rt, par=par: e.tensor_scalar(
                    out=ub[:, par * D:(par + 1) * D], in0=xr,
                    scalar1=st_scl[:, rt:rt + 1], scalar2=st_bia[:, rt:rt + 1],
                    op0=ALU.mult, op1=ALU.add).then_inc(sV.h, 1))
                sV.inc()
                em("vector", lambda e, par=par: e.tensor_scalar(
                    out=xt[:, par * D:(par + 1) * D], in0=ub[:, par * D:(par + 1) * D],
                    scalar1=-BMAG, scalar2=None, op0=ALU.add).then_inc(sV.h, 1))
                sV.inc()
            else:
                if rt >= 2:
                    wait("scalar", sV, vexp[rt - 2])
                em("scalar", lambda e, xr=xr, rt=rt, par=par: e.activation(
                    out=ub[:, par * D:(par + 1) * D], in_=xr, func=AF.Identity,
                    scale=st_istd[:, rt:rt + 1], bias=st_bia[:, rt:rt + 1]).then_inc(sS.h, 1))
                sS.inc()
                wait("vector", sS, sS.v)
                if rt >= 2:
                    wait("vector", sT, tdone[rt - 2])
                em("vector", lambda e, par=par: e.tensor_copy(
                    out=xt[:, par * D:(par + 1) * D],
                    in_=ub[:, par * D:(par + 1) * D]).then_inc(sV.h, 1))
                sV.inc()
            vexp[rt] = sV.v
            if tap and rt == 0:
                wait("gpsimd", sV, vexp[0])
                em("gpsimd", lambda e: e.dma_start(out=d_dbg_xt, in_=xt[:, 0:D]).then_inc(sLg.h, 16))
                sLg.inc(16)
                em("gpsimd", lambda e: e.dma_start(out=d_dbg_ub, in_=ub[:, 0:D]).then_inc(sLg.h, 16))
                sLg.inc(16)
                em("gpsimd", lambda e: e.dma_start(out=d_dbg_xs, in_=x_sb[:, 0:4 * D]).then_inc(sLg.h, 16))
                sLg.inc(16)
                wait("vector", sLg, sLg.v)
                wait("scalar", sLg, sLg.v)
            wait("tensor", sV, vexp[rt])
            for hb in range(2):
                if rt >= 1:
                    wait("tensor", sV, evd[(rt - 1, hb)])
                for it8 in range(8):
                    it = hb * 8 + it8
                    em("tensor", lambda e, par=par, it=it, hb=hb, it8=it8: e.transpose(
                        ptr[:, hb * 1024 + it8 * 128: hb * 1024 + (it8 + 1) * 128],
                        xt[:, par * D + it * 128: par * D + (it + 1) * 128],
                        c_eye[:]).then_inc(sT.h, 1) if it8 == 7 else e.transpose(
                        ptr[:, hb * 1024 + it8 * 128: hb * 1024 + (it8 + 1) * 128],
                        xt[:, par * D + it * 128: par * D + (it + 1) * 128],
                        c_eye[:]))
                sT.inc()
                wait("vector", sT, sT.v)
                em("vector", lambda e, rt=rt, hb=hb: e.tensor_copy(
                    out=dstT[:].rearrange("p (t r) -> p t r", t=NIT)[:, hb * 8:(hb + 1) * 8,
                                                                    rt * 128:(rt + 1) * 128],
                    in_=ptr[:].rearrange("p (h c) -> p h c", h=2)[:, hb:hb + 1]).then_inc(sV.h, 1))
                sV.inc()
                evd[(rt, hb)] = sV.v
            tdone[rt] = sT.v
        barrier()
        st.close()

    def p_ffn(d_src, d_dst, xi1T):
        pst = ExitStack()
        hid = mk(pst, [128, 4 * DFF], F32)
        gb = mk(pst, [128, 2 * CW], F32)
        ubv = mk(pst, [128, 2 * CW], F32)
        def _half(h):
            hst = ExitStack()
            gst = ExitStack()
            wgb = mk(gst, [128, 2 * NIT * CW], BF16)
            wub = mk(gst, [128, 2 * NIT * CW], BF16)
            pg = mk(gst, [128, 2 * 512], F32, psum=True)
            pu = mk(gst, [128, 2 * 512], F32, psum=True)
            mmdone = {}
            evdone = {}
            for c in range(NCH):
                par = c % 2
                if c >= 2:
                    wait("sync", sT, mmdone[c - 2])
                cc0 = c * CW
                for src_d, buf, csem in ((d_agg_o, wgb, sCu), (d_agu_o, wub, sCd)):
                    if c == 0 and h == 0:
                        wait("sync", csem, 1)
                    for s4 in range(4):
                        em("sync", lambda e, src_d=src_d, buf=buf, par=par, cc0=cc0, s4=s4:
                           e.dma_start(
                               out=buf[:].rearrange("p (b t c) -> p b t c", b=2, t=NIT)[:, par, 4 * s4:4 * s4 + 4],
                               in_=src_d.rearrange("(t p) c -> p t c", p=128)[:, 4 * s4:4 * s4 + 4, cc0:cc0 + CW]
                           ).then_inc(sLd.h, 16))
                        sLd.inc(16)
                wld = sLd.v
                for rt in range(4):
                    art = h * 4 + rt
                    wait("tensor", sLd, wld)
                    if (c, rt) != (0, 0):
                        prev = (c, rt - 1) if rt > 0 else (c - 1, 3)
                        wait("tensor", sV, evdone[prev][0])
                        wait("tensor", sS, evdone[prev][1])
                    for wi, (bufsel, psel) in enumerate(((wgb, pg), (wub, pu))):
                        for it in range(NIT):
                            fin = (wi == 1 and it == NIT - 1)
                            def _mm(e, it=it, art=art, par=par, bufsel=bufsel, psel=psel, fin=fin):
                                m = e.matmul(
                                    psel[:, par * 512: par * 512 + CW],
                                    xi1T[:].rearrange("p (t r) -> p t r", t=NIT)[:, it, art * 128:(art + 1) * 128],
                                    bufsel[:].rearrange("p (b t c) -> p b t c", b=2, t=NIT)[:, par, it],
                                    start=(it == 0), stop=(it == NIT - 1))
                                if fin:
                                    m.then_inc(sT.h, 1)
                            em("tensor", _mm)
                    sT.inc()
                    mmdone[c] = sT.v
                    wait("scalar", sT, sT.v)
                    em("scalar", lambda e, art=art, par=par: e.activation(
                        out=gb[:, par * CW:(par + 1) * CW],
                        in_=pg[:, par * 512: par * 512 + CW],
                        func=AF.Silu, scale=st_inv[:, art:art + 1]).then_inc(sS.h, 1))
                    sS.inc()
                    wait("vector", sT, sT.v)
                    em("vector", lambda e, art=art, par=par: e.tensor_scalar(
                        out=ubv[:, par * CW:(par + 1) * CW],
                        in0=pu[:, par * 512: par * 512 + CW],
                        scalar1=st_inv[:, art:art + 1], scalar2=None,
                        op0=ALU.mult).then_inc(sV.h, 1))
                    sV.inc()
                    wait("vector", sS, sS.v)
                    em("vector", lambda e, rt=rt, par=par, cc0=cc0: e.tensor_tensor(
                        out=hid[:, rt * DFF + cc0: rt * DFF + cc0 + CW],
                        in0=gb[:, par * CW:(par + 1) * CW],
                        in1=ubv[:, par * CW:(par + 1) * CW], op=ALU.mult).then_inc(sV.h, 1))
                    sV.inc()
                    evdone[(c, rt)] = (sV.v, sS.v)
            gate_mm_done = sT.v
            gst.close()
            for rt in range(4):
                em("vector", lambda e, rt=rt: e.tensor_reduce(
                    out=st_a[:, rt:rt + 1], in_=hid[:, rt * DFF:(rt + 1) * DFF], axis=AX,
                    op=ALU.max, apply_absolute_value=True).then_inc(sV.h, 1))
                sV.inc()
                wait("vector", sV, sV.v)

            def V(f):
                em("vector", f)
                sV.inc()
                wait("vector", sV, sV.v)

            V(lambda e: e.tensor_scalar(out=st_a[:, 0:4], in0=st_a[:, 0:4], scalar1=EPS,
                                        scalar2=None, op0=ALU.max).then_inc(sV.h, 1))
            V(lambda e, h=h: e.tensor_scalar(out=h_hi[:, 4 * h:4 * h + 4], in0=st_a[:, 0:4],
                                             scalar1=0.5 / 127, scalar2=None,
                                             op0=ALU.mult).then_inc(sV.h, 1))
            wait("scalar", sV, sV.v)
            em("scalar", lambda e: e.activation(out=st_b[:, 0:4], in_=st_a[:, 0:4],
                                                func=AF.Ln).then_inc(sS.h, 1))
            sS.inc()
            em("scalar", lambda e: e.activation(out=st_c[:, 0:4], in_=st_b[:, 0:4],
                                                func=AF.Exp, scale=-1.0).then_inc(sS.h, 1))
            sS.inc()
            wait("vector", sS, sS.v)
            V(lambda e: e.tensor_tensor(out=st_d[:, 0:4], in0=st_c[:, 0:4], in1=st_a[:, 0:4],
                                        op=ALU.mult).then_inc(sV.h, 1))
            V(lambda e: e.tensor_scalar(out=st_d[:, 0:4], in0=st_d[:, 0:4], scalar1=-1.0,
                                        scalar2=2.0, op0=ALU.mult, op1=ALU.add).then_inc(sV.h, 1))
            V(lambda e: e.tensor_tensor(out=st_b[:, 0:4], in0=st_c[:, 0:4], in1=st_d[:, 0:4],
                                        op=ALU.mult).then_inc(sV.h, 1))
            V(lambda e, h=h: e.tensor_scalar(out=h_s2[:, 4 * h:4 * h + 4], in0=st_b[:, 0:4],
                                             scalar1=127.0, scalar2=None,
                                             op0=ALU.mult).then_inc(sV.h, 1))
            xi2 = mk(hst, [128, DFF], BF16)
            xi2T = mk(hst, [128, 4 * NDT * 128], BF16)
            ptr2 = mk(hst, [128, 2 * 1024], BF16, psum=True)
            wait("vector", sT, gate_mm_done)
            rdone = {}
            ev2 = {}
            nb = 0
            for rt in range(4):
                V(lambda e, rt=rt, h=h: e.tensor_scalar(
                    out=hid[:, rt * DFF:(rt + 1) * DFF], in0=hid[:, rt * DFF:(rt + 1) * DFF],
                    scalar1=h_s2[:, 4 * h + rt:4 * h + rt + 1], scalar2=BMAG,
                    op0=ALU.mult, op1=ALU.add).then_inc(sV.h, 1))
                if rt >= 1:
                    wait("vector", sT, rdone[rt - 1])
                V(lambda e, rt=rt: e.tensor_scalar(
                    out=xi2[:], in0=hid[:, rt * DFF:(rt + 1) * DFF],
                    scalar1=-BMAG, scalar2=None, op0=ALU.add).then_inc(sV.h, 1))
                wait("tensor", sV, sV.v)
                for b0 in range(0, NDT, 8):
                    hb = nb % 2
                    cnt = min(8, NDT - b0)
                    if nb >= 2:
                        wait("tensor", sV, ev2[nb - 2])
                    for j in range(cnt):
                        def _tp(e, b0=b0, j=j, hb=hb, fin=(j == cnt - 1)):
                            m = e.transpose(
                                ptr2[:, hb * 1024 + j * 128: hb * 1024 + (j + 1) * 128],
                                xi2[:, (b0 + j) * 128:(b0 + j + 1) * 128],
                                c_eye[:])
                            if fin:
                                m.then_inc(sT.h, 1)
                        em("tensor", _tp)
                    sT.inc()
                    wait("vector", sT, sT.v)
                    em("vector", lambda e, rt=rt, b0=b0, hb=hb, cnt=cnt: e.tensor_copy(
                        out=xi2T[:, rt * NDT * 128 + b0 * 128: rt * NDT * 128 + (b0 + cnt) * 128],
                        in_=ptr2[:, hb * 1024: hb * 1024 + cnt * 128]).then_inc(sV.h, 1))
                    sV.inc()
                    ev2[nb] = sV.v
                    nb += 1
                rdone[rt] = sT.v
            wdb = mk(hst, [128, 2 * 512], BF16)
            xio = mk(hst, [128, 2 * 4 * 512], F32)
            pd = mk(hst, [128, 4 * 512], F32, psum=True)
            wait("sync", sT, gate_mm_done)
            wait("gpsimd", sT, gate_mm_done)
            dn_mm = {}
            dn_ev = {}
            xld = {}
            st_done = {}
            for dc in range(4):
                dpar = dc % 2
                if dc >= 2:
                    wait("gpsimd", sLg, st_done[dc - 2])
                for rt in range(4):
                    em("gpsimd", lambda e, rt=rt, h=h, dc=dc, dpar=dpar: e.dma_start(
                        out=xio[:, dpar * 2048 + rt * 512:dpar * 2048 + (rt + 1) * 512],
                        in_=d_src[(h * 4 + rt) * 128:(h * 4 + rt + 1) * 128,
                                  dc * 512:(dc + 1) * 512]).then_inc(sLg.h, 16))
                    sLg.inc(16)
                xld[dc] = sLg.v
                for dt in range(NDT):
                    par = dt % 2
                    if dt == 0 and dc == 0:
                        wait("sync", sCx, 1)
                    if dt >= 2:
                        wait("sync", sT, dn_mm[(dc, dt - 2)])
                    elif dc > 0:
                        wait("sync", sT, dn_mm[(dc - 1, NDT - 1 - dt)])
                    for half in range(2):
                        r = 2 * dc + half
                        em("sync", lambda e, r=r, dt=dt, par=par, half=half: e.dma_start(
                            out=wdb[:, par * 512 + half * 256: par * 512 + (half + 1) * 256],
                            in_=d_agd_o[r * DFF + dt * 128: r * DFF + (dt + 1) * 128, :]
                        ).then_inc(sLd.h, 16))
                        sLd.inc(16)
                    wait("tensor", sLd, sLd.v)
                    if dt == 0 and dc >= 1:
                        for rt in range(4):
                            wait("tensor", sV, dn_ev[(dc - 1, rt)])
                    for rt in range(4):
                        def _dm(e, rt=rt, dt=dt, par=par, fin=(rt == 3)):
                            m = e.matmul(
                                pd[:, rt * 512:(rt + 1) * 512],
                                xi2T[:, rt * NDT * 128 + dt * 128: rt * NDT * 128 + (dt + 1) * 128],
                                wdb[:, par * 512:(par + 1) * 512],
                                start=(dt == 0), stop=(dt == NDT - 1))
                            if fin:
                                m.then_inc(sT.h, 1)
                        em("tensor", _dm)
                    sT.inc()
                    dn_mm[(dc, dt)] = sT.v
                wait("vector", sT, dn_mm[(dc, NDT - 1)])
                wait("vector", sLg, xld[dc])
                for rt in range(4):
                    em("vector", lambda e, rt=rt, dc=dc, h=h, dpar=dpar: e.scalar_tensor_tensor(
                        out=xio[:, dpar * 2048 + rt * 512: dpar * 2048 + (rt + 1) * 512],
                        in0=pd[:, rt * 512:(rt + 1) * 512],
                        scalar=h_hi[:, 4 * h + rt:4 * h + rt + 1],
                        in1=xio[:, dpar * 2048 + rt * 512: dpar * 2048 + (rt + 1) * 512],
                        op0=ALU.mult, op1=ALU.add).then_inc(sV.h, 1))
                    sV.inc()
                    dn_ev[(dc, rt)] = sV.v
                wait("gpsimd", sV, sV.v)
                for rt in range(4):
                    em("gpsimd", lambda e, rt=rt, h=h, dc=dc, dpar=dpar: e.dma_start(
                        out=d_dst[(h * 4 + rt) * 128:(h * 4 + rt + 1) * 128, dc * 512:(dc + 1) * 512],
                        in_=xio[:, dpar * 2048 + rt * 512: dpar * 2048 + (rt + 1) * 512]
                    ).then_inc(sLg.h, 16))
                    sLg.inc(16)
                st_done[dc] = sLg.v
            barrier()
            hst.close()
        _half(0)
        _half(1)
        pst.close()

    def p_attn_prep(ast):
        QT = mk(ast, [128, NH * RPC], BF16)
        KTf = mk(ast, [128, 4 * N], BF16)
        Vf = mk(ast, [128, 16 * 512], BF16)
        lst = ExitStack()
        LN2T = mk(lst, [128, NIT * RPC], BF16)
        pk = mk(lst, [128, 2 * 512], F32, psum=True)
        p_ln(d_x1, False, LN2T)
        kst = ExitStack()
        wst = mk(kst, [128, 2 * 512], F32)
        Wk_sb = mk(kst, [128, NIT * 512], BF16)
        Wv_sb = mk(kst, [128, NIT * 512], BF16)
        cast_done = {}
        for wi, (d_w, dst) in enumerate(((d_Wk, Wk_sb), (d_Wv, Wv_sb))):
            for it in range(NIT):
                k = wi * NIT + it
                if k >= 2:
                    wait("sync", sV, cast_done[k - 2])
                em("sync", lambda e, d_w=d_w, it=it, k=k: e.dma_start(
                    out=wst[:, (k % 2) * 512:(k % 2 + 1) * 512],
                    in_=d_w[it * 128:(it + 1) * 128, :]).then_inc(sLd.h, 16))
                sLd.inc(16)
                wait("vector", sLd, sLd.v)
                em("vector", lambda e, dst=dst, it=it, k=k: e.tensor_copy(
                    out=dst[:, it * 512:(it + 1) * 512],
                    in_=wst[:, (k % 2) * 512:(k % 2 + 1) * 512]).then_inc(sV.h, 1))
                sV.inc()
                cast_done[k] = sV.v
        wait("tensor", sV, sV.v)
        kv_ev = {}
        k = 0
        KTo = mk(kst, [128, 4 * 1024], BF16)
        Vo = mk(kst, [128, 8 * 512], BF16)
        for g in range(4):
            for ch in range(2):
                par = k % 2
                if k >= 2:
                    wait("tensor", sS, kv_ev[k - 2])
                for it in range(NIT):
                    def _km(e, g=g, ch=ch, it=it, par=par, fin=(it == NIT - 1)):
                        m = e.matmul(
                            pk[:, par * 512:(par + 1) * 512],
                            Wk_sb[:, it * 512 + g * 128: it * 512 + (g + 1) * 128],
                            LN2T[:].rearrange("p (t r) -> p t r", t=NIT)[:, it, ch * 512:(ch + 1) * 512],
                            start=(it == 0), stop=(it == NIT - 1))
                        if fin:
                            m.then_inc(sT.h, 1)
                    em("tensor", _km)
                sT.inc()
                wait("scalar", sT, sT.v)
                em("scalar", lambda e, g=g, ch=ch, par=par: e.activation(
                    out=KTo[:, g * 1024 + ch * 512: g * 1024 + (ch + 1) * 512],
                    in_=pk[:, par * 512:(par + 1) * 512], func=AF.Identity).then_inc(sS.h, 1))
                sS.inc()
                kv_ev[k] = sS.v
                k += 1
        for rt in range(8):
            par = k % 2
            if k >= 2:
                wait("tensor", sS, kv_ev[k - 2])
            for it in range(NIT):
                def _vm(e, rt=rt, it=it, par=par, fin=(it == NIT - 1)):
                    m = e.matmul(
                        pk[:, par * 512:(par + 1) * 512],
                        LN2T[:].rearrange("p (t r) -> p t r", t=NIT)[:, it, rt * 128:(rt + 1) * 128],
                        Wv_sb[:, it * 512:(it + 1) * 512],
                        start=(it == 0), stop=(it == NIT - 1))
                    if fin:
                        m.then_inc(sT.h, 1)
                em("tensor", _vm)
            sT.inc()
            wait("scalar", sT, sT.v)
            em("scalar", lambda e, rt=rt, par=par: e.activation(
                out=Vo[:, rt * 512:(rt + 1) * 512],
                in_=pk[:, par * 512:(par + 1) * 512], func=AF.Identity).then_inc(sS.h, 1))
            sS.inc()
            kv_ev[k] = sS.v
            k += 1
        wait("gpsimd", sS, sS.v)
        em("gpsimd", lambda e: e.dma_start(
            out=d_kv_i[:, 0:KV_HALF].rearrange("o (g p c) -> p (o g) c", g=4, p=128, c=1024),
            in_=KTo[:].rearrange("p (g c) -> p g c", g=4)).then_inc(sLg.h, 16))
        sLg.inc(16)
        em("gpsimd", lambda e: e.dma_start(
            out=d_kv_i[:, KV_HALF:].rearrange("o (t p c) -> p (o t) c", t=8, p=128, c=512),
            in_=Vo[:].rearrange("p (t c) -> p t c", t=8)).then_inc(sLg.h, 16))
        sLg.inc(16)
        wait("gpsimd", sLg, sLg.v)
        em("gpsimd", lambda e: e.collective_compute(
            "AllGather", ALU.bypass,
            replica_groups=[[2 * i, 2 * i + 1] for i in range(NC // 2)],
            ins=[d_kv_i.ap().opt()], outs=[d_kv_o.ap().opt()]).then_inc(sCk.h, 1))
        sCk.inc()
        em("gpsimd", lambda e: e.collective_compute(
            "AllGather", ALU.bypass,
            replica_groups=[[2 * i, 2 * i + 1] for i in range(NC // 2)],
            ins=[d_dmk_i.ap().opt()], outs=[d_dmk_o.ap().opt()]).then_inc(sCy.h, 1))
        sCy.inc()
        kst.close()
        pack_done = sLg.v
        wqst = mk(lst, [128, D], F32)
        Wq_sb = mk(lst, [128, NIT * D], BF16)
        qc = {}
        wait("sync", sLg, pack_done)
        for it in range(NIT):
            if it >= 1:
                wait("sync", sV, qc[it - 1])
            em("sync", lambda e, it=it: e.dma_start(
                out=wqst[:],
                in_=d_Wq[it * 128:(it + 1) * 128, :]).then_inc(sLd.h, 16))
            sLd.inc(16)
            wait("vector", sLd, sLd.v)
            em("vector", lambda e, it=it: e.tensor_copy(
                out=Wq_sb[:, it * D:(it + 1) * D],
                in_=wqst[:]).then_inc(sV.h, 1))
            sV.inc()
            qc[it] = sV.v
        wait("tensor", sV, sV.v)
        wait("tensor", sS, sS.v)
        k = 0
        for hh in range(NH):
            for ch in range(2):
                par = k % 2
                if k >= 2:
                    wait("tensor", sS, kv_ev[("q", k - 2)])
                for it in range(NIT):
                    def _qm(e, hh=hh, ch=ch, it=it, par=par, fin=(it == NIT - 1)):
                        m = e.matmul(
                            pk[:, par * 512:(par + 1) * 512],
                            Wq_sb[:, it * D + hh * 128: it * D + (hh + 1) * 128],
                            LN2T[:].rearrange("p (t r) -> p t r", t=NIT)[:, it, ch * 512:(ch + 1) * 512],
                            start=(it == 0), stop=(it == NIT - 1))
                        if fin:
                            m.then_inc(sT.h, 1)
                    em("tensor", _qm)
                sT.inc()
                wait("scalar", sT, sT.v)
                em("scalar", lambda e, hh=hh, ch=ch, par=par: e.activation(
                    out=QT[:, hh * 1024 + ch * 512: hh * 1024 + (ch + 1) * 512],
                    in_=pk[:, par * 512:(par + 1) * 512], func=AF.Identity).then_inc(sS.h, 1))
                sS.inc()
                kv_ev[("q", k)] = sS.v
                k += 1
        wait("gpsimd", sT, sT.v)
        lst.close()
        wait("gpsimd", sCy, sCy.v)
        for kt in range(16):
            o, j = KT_OWNER[kt], KT_J[kt]
            for g in range(4):
                em("gpsimd", lambda e, o=o, j=j, g=g, kt=kt: e.dma_start(
                    out=KTf[:, g * N + kt * 128: g * N + (kt + 1) * 128],
                    in_=d_kv_o[:, 0:KV_HALF].rearrange(
                        "o (g p c) -> o g p c", g=4, p=128, c=1024)[o, g, :, j * 128:(j + 1) * 128]
                ).then_inc(sLg.h, 16))
                sLg.inc(16)
            em("gpsimd", lambda e, o=o, j=j, kt=kt: e.dma_start(
                out=Vf[:, kt * 512:(kt + 1) * 512],
                in_=d_kv_o[:, KV_HALF:].rearrange(
                    "o (t p c) -> o t p c", t=8, p=128, c=512)[o, j]
            ).then_inc(sLg.h, 16))
            sLg.inc(16)
        barrier()
        return QT, KTf, Vf

    def p_attn(ast, QT, KTf, Vf):
        pst = ExitStack()
        ctxn = mk(pst, [128, NH * RPC], BF16)
        lrows = mk(pst, [16, RPC], F32)
        ltmp = mk(pst, [1, RPC], F32)
        hp = ExitStack()
        c_msk = mk(hp, [128, 16 * 2 * 128], BF16)
        ATb = mk(hp, [128, 2 * 1024], BF16)
        psc = mk(hp, [128, 2 * 1024], F32, psum=True)
        pctx = mk(hp, [128, 1024], F32, psum=True)
        plC = mk(hp, [128, 1024], F32, psum=True)
        for kk in range(4):
            em("sync", lambda e, kk=kk: e.dma_start(
                out=c_msk[:].rearrange("p (k t c) -> p k t c", k=16, t=2)[:, 4 * kk:4 * kk + 4],
                in_=d_msk.rearrange("k t p c -> p k t c")[:, 4 * kk:4 * kk + 4]
            ).then_inc(sLd.h, 16))
            sLd.inc(16)
        msk_ld = sLd.v
        wait("vector", sLd, msk_ld)
        exp_done = {}
        tuse = {}
        hd_ev = {}
        lmv = {}
        for hh in range(NH):
            g = hh // 4
            if hh >= 1:
                wait("tensor", sS, hd_ev[hh - 1][0])
                wait("tensor", sV, hd_ev[hh - 1][1])
            for kt in range(16):
                cnt = MAXCNT[kt]
                cs = (8 - cnt) * 128
                par = kt % 2
                ki = hh * 16 + kt
                regs = [(cs, 512), (512, 1024)] if cs < 512 else [(cs, 1024)]
                if ki >= 2:
                    wait("tensor", sS, exp_done[ki - 2])
                for ri, (r0, r1) in enumerate(regs):
                    def _sm(e, g=g, kt=kt, hh=hh, r0=r0, r1=r1, par=par, fin=(ri == len(regs) - 1)):
                        m = e.matmul(
                            psc[:, par * 1024 + r0: par * 1024 + r1],
                            KTf[:, g * N + kt * 128: g * N + (kt + 1) * 128],
                            QT[:, hh * 1024 + r0: hh * 1024 + r1],
                            start=True, stop=True)
                        if fin:
                            m.then_inc(sT.h, 1)
                    em("tensor", _sm)
                sT.inc()
                wait("scalar", sT, sT.v)
                if ki >= 2:
                    wait("scalar", sT, tuse[ki - 2])
                em("scalar", lambda e, cs=cs, par=par: e.activation(
                    out=ATb[:, par * 1024 + cs: (par + 1) * 1024],
                    in_=psc[:, par * 1024 + cs: (par + 1) * 1024],
                    func=AF.Exp, scale=SCL).then_inc(sS.h, 1))
                sS.inc()
                exp_done[ki] = sS.v
                mw = min(256, cnt * 128)
                wait("vector", sS, sS.v)
                em("vector", lambda e, kt=kt, cs=cs, mw=mw, par=par: e.tensor_tensor(
                    out=ATb[:, par * 1024 + cs: par * 1024 + cs + mw],
                    in0=ATb[:, par * 1024 + cs: par * 1024 + cs + mw],
                    in1=c_msk[:, kt * 256: kt * 256 + mw], op=ALU.mult).then_inc(sV.h, 1))
                sV.inc()
                wait("tensor", sV, sV.v)
                for ri, (r0, r1) in enumerate(regs):
                    sf = (kt == 0)
                    so = (r0 < 512 and kt == 7) or (r0 >= 512 and kt == 15)
                    em("tensor", lambda e, g=g, kt=kt, r0=r0, r1=r1, par=par, sf=sf, so=so:
                       e.matmul(
                           pctx[:, r0:r1],
                           Vf[:, kt * 512 + g * 128: kt * 512 + (g + 1) * 128],
                           ATb[:, par * 1024 + r0: par * 1024 + r1],
                           start=sf, stop=so))
                    def _lm(e, kt=kt, r0=r0, r1=r1, par=par, sf=sf, so=so, fin=(ri == len(regs) - 1)):
                        m = e.matmul(
                            plC[:, r0:r1],
                            c_on128[:],
                            ATb[:, par * 1024 + r0: par * 1024 + r1],
                            start=sf, stop=so)
                        if fin:
                            m.then_inc(sT.h, 1)
                    em("tensor", _lm)
                sT.inc()
                tuse[ki] = sT.v
            wait("scalar", sT, sT.v)
            em("scalar", lambda e, hh=hh: e.activation(
                out=ctxn[:, hh * 1024:(hh + 1) * 1024], in_=pctx[:],
                func=AF.Identity).then_inc(sS.h, 1))
            sS.inc()
            wait("vector", sT, sT.v)
            if hh >= 1:
                wait("vector", sLg, lmv[hh - 1])
            em("vector", lambda e, hh=hh: e.tensor_copy(
                out=ltmp[:], in_=plC[0:1, :]).then_inc(sV.h, 1))
            sV.inc()
            wait("gpsimd", sV, sV.v)
            em("gpsimd", lambda e, hh=hh: e.dma_start(
                out=lrows[hh:hh + 1, :], in_=ltmp[:]).then_inc(sLg.h, 16))
            sLg.inc(16)
            lmv[hh] = sLg.v
            hd_ev[hh] = (sS.v, sV.v)
        pass

        barrier()
        hp.close()
        wost = mk(pst, [128, D], F32)
        Wo_sb = mk(pst, [128, NIT * D], BF16)
        nst = ExitStack()
        sel16 = mk(nst, [16, 16 * 128], F32)
        lt1 = mk(nst, [16, RPC], F32)
        lt2 = mk(nst, [16, RPC], F32)
        em("sync", lambda e: e.dma_start(out=sel16[:], in_=d_sel).then_inc(sLd.h, 16))
        sLd.inc(16)
        sel_ld = sLd.v
        prb = mk(nst, [128, 2 * 1024], F32, psum=True)
        wait("scalar", sLg, lmv[NH - 1])
        em("scalar", lambda e: e.activation(out=lt1[:], in_=lrows[:],
                                            func=AF.Ln).then_inc(sS.h, 1))
        sS.inc()
        em("scalar", lambda e: e.activation(out=lt2[:], in_=lt1[:],
                                            func=AF.Exp, scale=-1.0).then_inc(sS.h, 1))
        sS.inc()
        wait("vector", sS, sS.v)
        em("vector", lambda e: e.tensor_tensor(out=lt1[:], in0=lt2[:], in1=lrows[:],
                                               op=ALU.mult).then_inc(sV.h, 1))
        sV.inc()
        wait("vector", sV, sV.v)
        em("vector", lambda e: e.tensor_scalar(out=lt1[:], in0=lt1[:], scalar1=-1.0,
                                               scalar2=2.0, op0=ALU.mult, op1=ALU.add).then_inc(sV.h, 1))
        sV.inc()
        wait("vector", sV, sV.v)
        em("vector", lambda e: e.tensor_tensor(out=lrows[:], in0=lt2[:], in1=lt1[:],
                                               op=ALU.mult).then_inc(sV.h, 1))
        sV.inc()
        wait("vector", sV, sV.v)
        rec_done = sV.v
        cast2 = {}
        for it in range(NIT):
            if it >= 1:
                wait("sync", sV, cast2[it - 1])
            em("sync", lambda e, it=it: e.dma_start(
                out=wost[:],
                in_=d_Wo[it * 128:(it + 1) * 128, :]).then_inc(sLd.h, 16))
            sLd.inc(16)
            wait("vector", sLd, sLd.v)
            em("vector", lambda e, it=it: e.tensor_copy(
                out=Wo_sb[:, it * D:(it + 1) * D],
                in_=wost[:]).then_inc(sV.h, 1))
            sV.inc()
            cast2[it] = sV.v
        nrm = {}
        wait("tensor", sLd, sel_ld)
        wait("tensor", sV, rec_done)
        for hh in range(NH):
            par = hh % 2
            if hh >= 2:
                wait("tensor", sV, nrm[hh - 2])
            for ch in range(2):
                def _pm(e, hh=hh, ch=ch, par=par, fin=(ch == 1)):
                    m = e.matmul(
                        prb[:, par * 1024 + ch * 512: par * 1024 + (ch + 1) * 512],
                        sel16[:, hh * 128:(hh + 1) * 128],
                        lrows[:, ch * 512:(ch + 1) * 512],
                        start=True, stop=True)
                    if fin:
                        m.then_inc(sT.h, 1)
                em("tensor", _pm)
            sT.inc()
            wait("vector", sT, sT.v)
            em("vector", lambda e, hh=hh, par=par: e.tensor_tensor(
                out=ctxn[:, hh * 1024:(hh + 1) * 1024],
                in0=ctxn[:, hh * 1024:(hh + 1) * 1024],
                in1=prb[:, par * 1024:(par + 1) * 1024], op=ALU.mult).then_inc(sV.h, 1))
            sV.inc()
            nrm[hh] = sV.v
        norm_all = sV.v
        barrier()
        nst.close()
        po = mk(pst, [128, 2 * 512], F32, psum=True)
        xq = mk(pst, [128, 2 * D], F32)
        wait("tensor", sV, norm_all)
        wait("tensor", sV, cast2[NIT - 1])
        ao_ev = {}
        xq_st = {}
        for qt in range(8):
            par = qt % 2
            if qt >= 2:
                wait("gpsimd", sLg, xq_st[qt - 2])
            em("gpsimd", lambda e, qt=qt, par=par: e.dma_start(
                out=xq[:, par * D:(par + 1) * D],
                in_=d_x1[qt * 128:(qt + 1) * 128, :]).then_inc(sLg.h, 16))
            sLg.inc(16)
            wait("vector", sLg, sLg.v)
            em("vector", lambda e, par=par: e.tensor_tensor(
                out=xq[:, par * D:(par + 1) * D], in0=xq[:, par * D:(par + 1) * D],
                in1=c_bo[:], op=ALU.add).then_inc(sV.h, 1))
            sV.inc()
            bo_done = sV.v
            for oc in range(4):
                opar = oc % 2
                if (qt, oc) > (0, 1):
                    prev = (qt, oc - 2) if oc >= 2 else (qt - 1, oc + 2)
                    wait("tensor", sV, ao_ev[prev])
                for hh in range(NH):
                    def _om(e, hh=hh, qt=qt, oc=oc, opar=opar, fin=(hh == NH - 1)):
                        m = e.matmul(
                            po[:, opar * 512:(opar + 1) * 512],
                            ctxn[:, hh * 1024 + qt * 128: hh * 1024 + (qt + 1) * 128],
                            Wo_sb[:, hh * D + oc * 512: hh * D + (oc + 1) * 512],
                            start=(hh == 0), stop=(hh == NH - 1))
                        if fin:
                            m.then_inc(sT.h, 1)
                    em("tensor", _om)
                sT.inc()
                wait("vector", sT, sT.v)
                wait("vector", sV, bo_done)
                em("vector", lambda e, qt=qt, oc=oc, par=par, opar=opar: e.tensor_tensor(
                    out=xq[:, par * D + oc * 512: par * D + (oc + 1) * 512],
                    in0=po[:, opar * 512:(opar + 1) * 512],
                    in1=xq[:, par * D + oc * 512: par * D + (oc + 1) * 512],
                    op=ALU.add).then_inc(sV.h, 1))
                sV.inc()
                ao_ev[(qt, oc)] = sV.v
            wait("gpsimd", sV, ao_ev[(qt, 3)])
            em("gpsimd", lambda e, qt=qt, par=par: e.dma_start(
                out=d_x2[qt * 128:(qt + 1) * 128, :],
                in_=xq[:, par * D:(par + 1) * D]).then_inc(sLg.h, 16))
            sLg.inc(16)
            xq_st[qt] = sLg.v
        barrier()
        pst.close()

    import os as _os
    _stage = int(_os.environ.get("KBISECT", "99"))
    pro = ExitStack()
    p_consts(pro)
    if _stage >= 1:
        p_quant(pro)
    barrier()
    pro.close()
    if _stage >= 2:
        f1 = ExitStack()
        xi1T_a = mk(f1, [128, NIT * RPC], BF16)
        p_ln(d_x, True, xi1T_a, tap=(_dbgenv in ('1', '3')))
        if _dbgenv in ('1', '4'):
            em("gpsimd", lambda e: e.dma_start(out=d_dbg_xi, in_=xi1T_a[:]).then_inc(sLg.h, 16))
            sLg.inc(16)
            def _dst(e):
                e.dma_start(out=d_dbg_st[:, 0:8], in_=st_scl[:]).then_inc(sLg.h, 16)
                e.dma_start(out=d_dbg_st[:, 8:16], in_=st_bia[:]).then_inc(sLg.h, 16)
                e.dma_start(out=d_dbg_st[:, 16:24], in_=st_inv[:]).then_inc(sLg.h, 16)
                e.dma_start(out=d_dbg_st[:, 24:32], in_=st_istd[:]).then_inc(sLg.h, 16)
            em("gpsimd", _dst)
            sLg.inc(64)
            barrier()
        if _stage >= 3:
            p_ffn(d_x, d_x1, xi1T_a)
        f1.close()
    if _stage >= 4:
        ast = ExitStack()
        QT, KTf, Vf = p_attn_prep(ast)
        if _stage >= 5:
            p_attn(ast, QT, KTf, Vf)
        ast.close()
    if _stage >= 6:
        f2 = ExitStack()
        xi1T_b = mk(f2, [128, NIT * RPC], BF16)
        p_ln(d_x2, True, xi1T_b)
        p_ffn(d_x2, d_y, xi1T_b)
        f2.close()
    if _dbg:
        wait("gpsimd", sLg, sLg.v)
        em("gpsimd", lambda e: e.dma_start(out=d_dbg_x1, in_=d_x1[:, :]).then_inc(sLg.h, 16))
        sLg.inc(16)
        em("gpsimd", lambda e: e.dma_start(out=d_dbg_x2, in_=d_x2[:, :]).then_inc(sLg.h, 16))
        sLg.inc(16)
        em("gpsimd", lambda e: e.dma_start(out=d_dbg_wg, in_=d_agg_o[:, :]).then_inc(sLg.h, 16))
        sLg.inc(16)
        em("gpsimd", lambda e: e.dma_start(out=d_dbg_wd, in_=d_agd_o[:, :]).then_inc(sLg.h, 16))
        sLg.inc(16)
    wait("gpsimd", sLg, sLg.v)
    wait("gpsimd", sLd, sLd.v)

    with nc.Block() as block:
        @block.sync
        def _(e):
            for f in ops["sync"]:
                f(e)

        @block.gpsimd
        def _(e):
            for f in ops["gpsimd"]:
                f(e)

        @block.tensor
        def _(e):
            for f in ops["tensor"]:
                f(e)

        @block.vector
        def _(e):
            for f in ops["vector"]:
                f(e)

        @block.scalar
        def _(e):
            for f in ops["scalar"]:
                f(e)

    es.close()
    return nc


def _build_masks(par):
    pos = POS_EVEN if par == 0 else POS_ODD
    tri = np.triu(np.ones((128, 128), np.float32))
    m = np.zeros((16, 2, 128, 128), np.float32)
    for kt in range(16):
        cnt = MAXCNT[kt]
        suffix = pos[8 - cnt:]
        for j in range(2):
            if j >= cnt:
                m[kt, j] = 1.0
                continue
            p = suffix[j]
            if p < kt:
                m[kt, j] = 0.0
            elif p == kt:
                m[kt, j] = tri
            else:
                m[kt, j] = 1.0
    return m.astype(BF)


_cache = {}


def kernel(L0_attn_x, L0_attn_Wq, L0_attn_Wk, L0_attn_Wv, L0_attn_Wo, L0_attn_bo,
           L0_ffn_w_gate, L0_ffn_w_up, L0_ffn_w_down):
    x = np.asarray(L0_attn_x, np.float32)
    Wq = np.ascontiguousarray(np.asarray(L0_attn_Wq, np.float32))
    Wk = np.ascontiguousarray(np.asarray(L0_attn_Wk, np.float32))
    Wv = np.ascontiguousarray(np.asarray(L0_attn_Wv, np.float32))
    Wo = np.ascontiguousarray(np.asarray(L0_attn_Wo, np.float32))
    bo = np.asarray(L0_attn_bo, np.float32).reshape(1, D)
    wgT = np.ascontiguousarray(np.asarray(L0_ffn_w_gate, np.float32).T)
    wuT = np.ascontiguousarray(np.asarray(L0_ffn_w_up, np.float32).T)
    wdT = np.ascontiguousarray(np.asarray(L0_ffn_w_down, np.float32).T)
    eye = np.eye(128, dtype=BF)
    on128 = np.ones((128, 128), BF)
    onKK = np.ones((128, 128), np.float32)
    on1 = np.ones((1, 128), np.float32)
    sel16 = np.zeros((16, 16 * 128), np.float32)
    for hh in range(16):
        sel16[hh, hh * 128:(hh + 1) * 128] = 1.0
    xr = x.reshape(B, 16, 128, D)
    in_maps = []
    for c in range(NC):
        b, par = c // 2, c % 2
        pos = POS_EVEN if par == 0 else POS_ODD
        in_maps.append({
            "x_rows": np.ascontiguousarray(xr[b, pos].reshape(RPC, D)),
            "wg_s": np.ascontiguousarray(wgT[256 * c:256 * (c + 1)]),
            "wu_s": np.ascontiguousarray(wuT[256 * c:256 * (c + 1)]),
            "wd_s": np.ascontiguousarray(wdT[:, 256 * c:256 * (c + 1)]),
            "Wq": Wq, "Wk": Wk, "Wv": Wv, "Wo": Wo, "bo": bo,
            "masks": _build_masks(par),
            "eye": eye, "ones128": on128, "onesKK": onKK, "ones1": on1,
            "sel16": sel16,
        })
    if "nc" not in _cache:
        _cache["nc"] = build_program()
    res = run_bass_kernel_spmd(_cache["nc"], in_maps, core_ids=list(range(NC)))
    y = np.empty((B, 16, 128, D), np.float32)
    for c in range(NC):
        b, par = c // 2, c % 2
        pos = POS_EVEN if par == 0 else POS_ODD
        y[b, pos] = res.results[c]["y_rows"].reshape(8, 128, D)
    return y.reshape(B, N, D)
